# version 10
# speedup vs baseline: 1.4595x; 1.0277x over previous
import numpy as np
import ml_dtypes
import concourse.bass as bass
import concourse.mybir as mybir

F32 = mybir.dt.float32
BF16 = mybir.dt.bfloat16
I32 = mybir.dt.int32
ALU = mybir.AluOpType
AF = mybir.ActivationFunctionType
AX = mybir.AxisListType

S = 2048
E = 2048
H = 16
D = 128
NCH = 16
PREMEAN_MODE = "full"
INV_SQRT_D = 1.0 / float(np.sqrt(np.float32(D), dtype=np.float32))


class MK:
    def __init__(self, debug=()):
        self.debug = set(debug)
        nc = bass.Bass("TRN2", target_bir_lowering=False, detect_race_conditions=False)
        self.nc = nc

        di = lambda n, sh, dt: nc.dram_tensor(n, sh, dt, kind="ExternalInput")
        self.xT = di("xT", [E, S], F32)
        self.w = {k: di("w_" + k, [E, E], F32) for k in ("q", "k", "v", "o", "f")}
        self.bias_col = di("bias_col", [128, NCH], F32)
        self.gamma_col = di("gamma_col", [128, NCH], F32)
        self.beta_col = di("beta_col", [128, NCH], F32)
        self.ones_row = di("ones_row", [1, 128], F32)
        self.ones_col_b = di("ones_col_b", [128, 1], BF16)
        self.ones_col_f = di("ones_col_f", [128, 1], F32)
        self.ident_f = di("ident_f", [128, 128], F32)
        self.ident_b = di("ident_b", [128, 128], BF16)
        self.c07 = di("c07_col", [128, 1], F32)

        self.y = nc.dram_tensor("y", [E, S], F32, kind="ExternalOutput")
        dk = "ExternalOutput" if "scratch" in self.debug else "Internal"
        self.QT_d = nc.dram_tensor("QT_d", [E, S], BF16, kind=dk)
        self.VT_d = nc.dram_tensor("VT_d", [E, S], BF16, kind=dk)
        self.Y2_d = nc.dram_tensor("Y2_d", [E, S], F32, kind=dk)
        self.rowtmp_d = nc.dram_tensor("rowtmp_d", [1, 2048], F32, kind="Internal")
        self.taps = {}

        sb = nc.alloc_sbuf_tensor
        self.ABF = sb("ABF", [128, NCH * S], BF16)
        self.BBF = sb("BBF", [128, NCH * S], BF16)
        self.CF32 = sb("CF32", [128, 8192], F32)
        self.P3T = sb("P3T", [128, 10496], BF16)
        self.I32T = sb("I32T", [128, 1024], I32)
        self.REP = sb("REP", [128, 4096], F32)
        self.stg = sb("stg", [128, 1024], BF16)
        self.sc = sb("sc", [1, 64], F32)
        self.cols = sb("cols", [128, 64], F32)
        self.sxcol = sb("sxcol", [128, 16], F32)
        self.so2col = sb("so2col", [128, 16], F32)
        self.lncol = sb("lncol", [128, 48], F32)
        self.consts = sb("consts", [128, 8], F32)
        self.ones_row_s = sb("ones_row_s", [1, 128], F32)
        self.ones_colb_s = sb("ones_colb_s", [128, 1], BF16)
        self.ones_colf_s = sb("ones_colf_s", [128, 1], F32)
        self.ident_s = sb("ident_s", [128, 128], F32)
        self.identb_s = sb("identb_s", [128, 128], BF16)
        self.tri_s = sb("tri_s", [128, 128], BF16)
        self.io_s = sb("io_s", [128, 128], I32)
        self.kmax_s = sb("kmax_s", [128, 4], F32)
        self.Ml_s = sb("Ml_s", [128, 192], F32)

        ps = nc.alloc_psum_tensor
        self.PS = [ps(f"PS{i}", [128, 512], F32) for i in range(2)]
        self.PSB = ps("PSB", [128, 512], BF16)
        self.PO = [ps(f"PO{i}", [128, 512], F32) for i in range(4)]
        self.PX = ps("PX", [128, 512], F32)

        als = nc.alloc_semaphore
        self.s_in = als("s_in");  self.n_in = 0
        self.s_gp = als("s_gp");  self.n_gp = 0
        self.s_te = als("s_te");  self.n_te = 0
        self.s_act = als("s_act"); self.n_act = 0
        self.s_dve = als("s_dve"); self.n_dve = 0
        self.s_gpe = als("s_gpe"); self.n_gpe = 0
        self.s_slot = [als(f"s_sl{i}") for i in range(8)]
        self.n_slot = [0] * 8
        self.s_gslot = [als(f"s_gsl{i}") for i in range(4)]
        self.n_gslot = [0] * 4

    def vsync(self, v):
        v.drain()

    def asynch(self, a):
        a.drain()

    def tap(self, name, ap, shape, dtype):
        if name not in self.debug:
            return
        t = self.nc.dram_tensor("tap_" + name, shape, dtype, kind="ExternalOutput")
        with self.nc.Block() as blk:
            @blk.sync
            def _(sync):
                sync.dma_start(t[:], ap).then_inc(self.s_in, 16)
                self.n_in += 16
                sync.wait_ge(self.s_in, self.n_in)

    def chunk(self, buf, i, w=2048):
        return buf[:, i * w:(i + 1) * w]

    def p0(self):
        nc = self.nc
        with nc.Block() as blk:
            @blk.sync
            def _(sync):
                for dst, src in ((self.consts[:, 0:1], self.c07),
                                 (self.ones_row_s[:], self.ones_row),
                                 (self.ones_colb_s[:], self.ones_col_b),
                                 (self.ones_colf_s[:], self.ones_col_f),
                                 (self.ident_s[:], self.ident_f),
                                 (self.identb_s[:], self.ident_b),
                                 (self.lncol[:, 0:16], self.bias_col),
                                 (self.lncol[:, 16:32], self.gamma_col),
                                 (self.lncol[:, 32:48], self.beta_col)):
                    sync.dma_start(dst, src[:]).then_inc(self.s_in, 16)
                    self.n_in += 16
                sync.wait_ge(self.s_in, self.n_in)

            @blk.gpsimd
            def _(gp):
                gp.iota(self.io_s[:], [[1, 128]], channel_multiplier=-1,
                        allow_small_or_imprecise_dtypes=True)

        with nc.Block() as blk:
            @blk.vector
            def _(v):
                C = self.cols
                v.tensor_copy(self.CF32[:, 0:128],
                              self.consts[:, 0:1].broadcast_to([128, 128]))
                self.vsync(v)
                v.tensor_copy(self.I32T[:, 0:128], self.CF32[:, 0:128])
                self.vsync(v)
                v.tensor_copy(self.CF32[:, 128:256], self.I32T[:, 0:128])
                self.vsync(v)
                v.tensor_copy(C[:, 0:1], self.CF32[:, 128:129])
                self.vsync(v)
                v.tensor_scalar(C[:, 1:2], C[:, 0:1], -256.5, 256.5, ALU.mult, ALU.add)
                v.tensor_scalar(C[:, 2:3], C[:, 0:1], 256.0, -256.0, ALU.mult, ALU.add)
                self.vsync(v)
                v.tensor_scalar(C[:, 3:4], C[:, 2:3], -1.0, -1.0, ALU.mult, ALU.add)
                v.tensor_scalar(C[:, 4:5], C[:, 2:3], -1.0, 1.0, ALU.mult, ALU.add)
                v.tensor_scalar(self.tri_s[:], self.io_s[:], 0.0, None, ALU.is_le)
                self.vsync(v)
        self.tap("cols", self.cols[:, 0:5], [128, 5], F32)

    def p1(self):
        nc = self.nc
        C = self.CF32
        xc = [C[:, 0:2048], C[:, 2048:4096]]
        rmax = C[:, 4096:6144]
        rmin = C[:, 6144:8192]

        b_sl = list(self.n_slot)
        b_dve = self.n_dve
        with nc.Block() as blk:
            @blk.sync
            def _(sync):
                for ec in range(NCH):
                    if ec >= 2:
                        sync.wait_ge(self.s_dve, b_dve + ec - 1)
                    sync.dma_start(xc[ec % 2], self.xT[ec * 128:(ec + 1) * 128, :])\
                        .then_inc(self.s_slot[ec % 2], 16)
                    self.n_slot[ec % 2] += 16

            @blk.vector
            def _(v):
                for ec in range(NCH):
                    v.wait_ge(self.s_slot[ec % 2], b_sl[ec % 2] + 16 * (ec // 2 + 1))
                    if ec == 0:
                        v.tensor_copy(rmax, xc[0])
                        v.tensor_copy(rmin, xc[0]).then_inc(self.s_dve, 1)
                    else:
                        v.tensor_tensor(rmax, rmax, xc[ec % 2], ALU.max)
                        v.tensor_tensor(rmin, rmin, xc[ec % 2], ALU.min)\
                            .then_inc(self.s_dve, 1)
                    self.n_dve += 1
                v.tensor_scalar_mul(rmin, rmin, -1.0)
                v.tensor_tensor(rmax, rmax, rmin, ALU.max).then_inc(self.s_dve, 1)
                self.n_dve += 1

        b_te, b_dve = self.n_te, self.n_dve
        with nc.Block() as blk:
            @blk.tensor
            def _(t):
                for i in range(NCH):
                    if i >= 1:
                        t.wait_ge(self.s_dve, b_dve + i)
                    t.transpose(self.PX[:, 0:128], rmax[:, i * 128:(i + 1) * 128],
                                self.ident_s[:]).then_inc(self.s_te, 1)
                    self.n_te += 1

            @blk.vector
            def _(v):
                for i in range(NCH):
                    v.wait_ge(self.s_te, b_te + i + 1)
                    v.tensor_reduce(self.sxcol[:, i:i + 1], self.PX[:, 0:128],
                                    AX.X, ALU.max).then_inc(self.s_dve, 1)
                    self.n_dve += 1

        rA = C[0:1, 0:2048]
        rB = C[0:1, 2048:4096]
        rC = C[0:1, 4096:6144]
        with nc.Block() as blk:
            @blk.sync
            def _(sync):
                rt = self.rowtmp_d[:].rearrange("o (c p) -> p (o c)", p=128, c=16)
                sync.dma_start(rt, self.sxcol[:], allow_slow_non_contiguous=True).then_inc(self.s_in, 16)
                self.n_in += 16
                sync.wait_ge(self.s_in, self.n_in)
                sync.dma_start(rA, self.rowtmp_d[:]).then_inc(self.s_in, 16)
                self.n_in += 16
                sync.wait_ge(self.s_in, self.n_in)
        with nc.Block() as blk:
            @blk.vector
            def _(v):
                v.tensor_scalar_max(rA, rA, 1e-5)
                v.reciprocal(rB, rA)
                v.tensor_scalar_mul(rB, rB, 127.0)
                v.tensor_scalar_mul(rC, rA, 1.0 / 127.0)
        self.replicate_row(rB, self.REP[:, 0:2048])
        self.replicate_row(rC, self.REP[:, 2048:4096])

        val = {}
        vsl = list(self.n_slot)
        vdve, vact = self.n_dve, self.n_act
        for ec in range(NCH):
            vsl[ec % 2] += 16
            val["ld", ec] = vsl[ec % 2]
            vdve += 1
            val["mul", ec] = vdve
            for hh in range(2):
                vact += 1
                val["ci", ec, hh] = vact
                vact += 1
                val["co", ec, hh] = vact
        self.n_slot = vsl
        self.n_dve, self.n_act = vdve, vact
        tb = [rmax, rmin]
        with nc.Block() as blk:
            @blk.sync
            def _(sync):
                for ec in range(NCH):
                    if ec >= 2:
                        sync.wait_ge(self.s_dve, val["mul", ec - 2])
                    sync.dma_start(xc[ec % 2], self.xT[ec * 128:(ec + 1) * 128, :])\
                        .then_inc(self.s_slot[ec % 2], 16)

            @blk.vector
            def _(v):
                for ec in range(NCH):
                    v.wait_ge(self.s_slot[ec % 2], val["ld", ec])
                    if ec >= 2:
                        v.wait_ge(self.s_act, val["ci", ec - 2, 1])
                    xch = xc[ec % 2]
                    v.tensor_tensor(tb[ec % 2][:, 0:1024], xch[:, 0:1024],
                                    self.REP[:, 0:1024], ALU.mult)
                    v.tensor_tensor(tb[ec % 2][:, 1024:2048], xch[:, 1024:2048],
                                    self.REP[:, 1024:2048], ALU.mult)\
                        .then_inc(self.s_dve, 1)

            @blk.scalar
            def _(a):
                for ec in range(NCH):
                    a.wait_ge(self.s_dve, val["mul", ec])
                    for hh in range(2):
                        sl = slice(hh * 1024, (hh + 1) * 1024)
                        a.activation(self.I32T[:], tb[ec % 2][:, sl], AF.Identity,
                                     bias=self.cols[:, 1:2])\
                            .then_inc(self.s_act, 1)
                        a.activation(self.chunk(self.ABF, ec)[:, sl], self.I32T[:],
                                     AF.Identity, bias=self.cols[:, 2:3])\
                            .then_inc(self.s_act, 1)

        self.tap("xq", self.ABF[:, 0:4096], [128, 4096], BF16)
        self.tap("sxmul", self.REP[:, 0:2048], [128, 2048], F32)

    def replicate_row(self, row_ap, dst_ap):
        nc = self.nc
        N = row_ap.shape[1]
        with nc.Block() as blk:
            @blk.tensor
            def _(t):
                for j in range(0, N, 512):
                    if j // 512 >= 2:
                        t.wait_ge(self.s_dve, self.n_dve + j // 512 - 1)
                    t.matmul(self.PS[(j // 512) % 2][:], self.ones_row_s[:],
                             row_ap[:, j:j + 512], start=True, stop=True)\
                        .then_inc(self.s_te, 1)
                    self.n_te += 1
            b_te = self.n_te - N // 512

            @blk.vector
            def _(v):
                for j in range(0, N, 512):
                    v.wait_ge(self.s_te, b_te + j // 512 + 1)
                    v.tensor_copy(dst_ap[:, j:j + 512], self.PS[(j // 512) % 2][:])\
                        .then_inc(self.s_dve, 1)
                    self.n_dve += 1

    def weight_mean(self, wkey, wi, kind):
        nc = self.nc
        C = self.CF32
        wd = self.w[wkey]
        xc = [C[:, 0:2048], C[:, 2048:4096]]
        ab = [self.P3T[:, 0:2048], self.P3T[:, 6144:8192]]
        b_sl = list(self.n_slot)
        b_act, b_te = self.n_act, self.n_te
        with nc.Block() as blk:
            @blk.sync
            def _(sync):
                for ec in range(NCH):
                    if ec >= 2:
                        sync.wait_ge(self.s_act, b_act + ec - 1)
                    sync.dma_start(xc[ec % 2], wd[ec * 128:(ec + 1) * 128, :])\
                        .then_inc(self.s_slot[ec % 2], 16)
                    self.n_slot[ec % 2] += 16

            @blk.scalar
            def _(a):
                for ec in range(NCH):
                    a.wait_ge(self.s_slot[ec % 2], b_sl[ec % 2] + 16 * (ec // 2 + 1))
                    if ec >= 2:
                        a.wait_ge(self.s_te, b_te + ec - 1)
                    a.activation(ab[ec % 2], xc[ec % 2], AF.Abs)\
                        .then_inc(self.s_act, 1)
                    self.n_act += 1

            @blk.tensor
            def _(t):
                for ec in range(NCH):
                    t.wait_ge(self.s_act, b_act + ec + 1)
                    for sb in range(4):
                        ins = t.matmul(self.PX[0:1, 0:512], self.ones_colb_s[:],
                                       ab[ec % 2][:, sb * 512:(sb + 1) * 512],
                                       start=(ec == 0 and sb == 0),
                                       stop=(ec == NCH - 1 and sb == 3))
                    ins.then_inc(self.s_te, 1)
                    self.n_te += 1

        with nc.Block() as blk:
            @blk.vector
            def _(v):
                v.reduce_sum(self.sc[0:1, 32:33], self.PX[0:1, 0:512], AX.X)
                self.vsync(v)
                v.tensor_scalar_mul(self.sc[0:1, 33:34], self.sc[0:1, 32:33],
                                    1.0 / float(E * E))
                self.vsync(v)
                if kind == "attn":
                    v.tensor_scalar_max(self.sc[0:1, wi:wi + 1],
                                        self.sc[0:1, 33:34], 1e-5)
                else:
                    v.tensor_scalar_add(self.sc[0:1, wi:wi + 1],
                                        self.sc[0:1, 33:34], 1e-5)
                self.vsync(v)
                v.reciprocal(self.sc[0:1, 34:35], self.sc[0:1, wi:wi + 1])
                self.vsync(v)
                v.tensor_copy(self.CF32[0:1, 5248:5376],
                              self.sc[0:1, 34:35].broadcast_to([1, 128]))\
                    .then_inc(self.s_dve, 1)
                self.n_dve += 1
        return self.sw_tail()

    def sw_tail(self):
        nc = self.nc
        b_te, b_dve = self.n_te, self.n_dve
        with nc.Block() as blk:
            @blk.tensor
            def _(t):
                t.wait_ge(self.s_dve, b_dve)
                t.matmul(self.PX[:, 0:128], self.ones_row_s[:],
                         self.CF32[0:1, 5248:5376],
                         start=True, stop=True).then_inc(self.s_te, 1)
                self.n_te += 1

            @blk.vector
            def _(v):
                v.wait_ge(self.s_te, b_te + 1)
                v.tensor_copy(self.CF32[:, 5120:5248], self.PX[:, 0:128])\
                    .then_inc(self.s_dve, 1)
                self.n_dve += 1
        return self.n_te, self.n_dve

    def mean_rows(self, wi, kind, pmm_val, replicate=True, save_sc=None):
        nc = self.nc
        with nc.Block() as blk:
            @blk.vector
            def _(v):
                v.wait_ge(self.s_dve, pmm_val)
                v.reduce_sum(self.sc[0:1, 32:33], self.CF32[0:1, 4096:4608],
                             AX.X)
                self.vsync(v)
                v.tensor_scalar_mul(self.sc[0:1, 33:34], self.sc[0:1, 32:33],
                                    1.0 / float(E * E))
                self.vsync(v)
                if kind == "attn":
                    v.tensor_scalar_max(self.sc[0:1, wi:wi + 1],
                                        self.sc[0:1, 33:34], 1e-5)
                else:
                    v.tensor_scalar_add(self.sc[0:1, wi:wi + 1],
                                        self.sc[0:1, 33:34], 1e-5)
                self.vsync(v)
                v.reciprocal(self.sc[0:1, 34:35], self.sc[0:1, wi:wi + 1])
                self.vsync(v)
                if save_sc is not None:
                    v.tensor_copy(self.sc[0:1, save_sc:save_sc + 1],
                                  self.sc[0:1, 34:35])
                    self.vsync(v)
                if replicate:
                    v.tensor_copy(self.CF32[0:1, 5248:5376],
                                  self.sc[0:1, 34:35].broadcast_to([1, 128]))\
                        .then_inc(self.s_dve, 1)
                else:
                    v.nop().then_inc(self.s_dve, 1)
                self.n_dve += 1
        if replicate:
            return self.sw_tail()
        return None, self.n_dve

    def weight_pass2(self, wkey, dest, premean=None, wait_pld=None,
                     wait_sw=None):
        nc = self.nc
        C = self.CF32
        dve_entry = self.n_dve
        wd = self.w[wkey]
        xc = [C[:, 0:2048], C[:, 2048:4096]]
        swW = C[:, 5120:5121]
        ychunk = C[:, 5248:7296]
        tmp512 = C[:, 7296:7808]
        xchunk = self.REP[:, 0:2048]
        wt = [self.P3T[:, 0:2048], self.P3T[:, 6144:8192]]
        pm = premean is not None
        pm_abs = pm and PREMEAN_MODE != "noabs"
        pm_mm = pm and PREMEAN_MODE not in ("noabs", "nomm")
        if pm:
            assert dest in ("K", "Q", "V")
            pwd = self.w[premean]
            pld = [C[:, 5248:7296], self.REP[:, 0:2048]]
            absb = self.P3T[:, 8448:10496]
            accrow = C[0:1, 4096:4608]
        NOC = 16
        val = {}
        vsl = list(self.n_slot)
        vgs = list(self.n_gslot)
        vact, vdve, vte = self.n_act, self.n_dve, self.n_te
        for oc in range(NOC):
            vsl[oc % 2] += 16
            val["load", oc] = vsl[oc % 2]
            if pm:
                vsl[5 + oc % 2] += 16
                val["pld", oc] = vsl[5 + oc % 2]
        for oc in range(NOC + 3):
            if pm_abs and oc >= 3 and oc - 3 < NOC:
                vact += 1
                val["pabs", oc - 3] = vact
            if oc < NOC:
                vact += 1
                val["act", oc] = vact
                for hh in range(2):
                    vact += 1
                    val["ci", oc, hh] = vact
                    vact += 1
                    val["co", oc, hh] = vact
        for oc in range(NOC + 3):
            if oc < NOC:
                vdve += 1
                val["clip", oc] = vdve
            if 1 <= oc <= NOC:
                for sb in range(4):
                    vte += 1
                    val["mm", oc - 1, sb] = vte
            if 2 <= oc <= NOC + 1:
                for sb in range(4):
                    vdve += 1
                    val["evac", oc - 2, sb] = vdve
                    if dest in ("Q", "V"):
                        k = (oc - 2) * 4 + sb
                        vgs[k % 2] += 16
                        val["stgdma", oc - 2, sb] = vgs[k % 2]
                if dest == "Y1":
                    vdve += 1
                    val["post", oc - 2] = vdve
                if dest == "Y2":
                    vgs[2 + (oc - 2) % 2] += 16
                    val["ydma", oc - 2] = vgs[2 + (oc - 2) % 2]
            if pm_mm and 2 <= oc and oc - 2 < NOC:
                vte += 1
                val["pmm", oc - 2] = vte
            if pm_mm and 3 <= oc and oc - 3 < NOC:
                vdve += 1
                val["pacc", oc - 3] = vdve
        if dest == "Y1":
            for oc in range(NOC):
                vsl[5] += 16
                val["xload", oc] = vsl[5]
        self.n_slot = vsl
        self.n_gslot = vgs
        self.n_act, self.n_dve, self.n_te = vact, vdve, vte

        with nc.Block() as blk:
            @blk.sync
            def _(sync):
                for oc in range(NOC):
                    if oc >= 2:
                        sync.wait_ge(self.s_act, val["ci", oc - 2, 1])
                    sync.dma_start(xc[oc % 2], wd[oc * 128:(oc + 1) * 128, :])\
                        .then_inc(self.s_slot[oc % 2], 16)

            @blk.scalar
            def _(a):
                for oc in range(NOC + 3):
                    if pm_abs and oc >= 3 and oc - 3 < NOC:
                        j = oc - 3
                        a.wait_ge(self.s_slot[5 + j % 2], val["pld", j])
                        if j >= 1 and pm_mm:
                            a.wait_ge(self.s_te, val["pmm", j - 1])
                        a.activation(absb, pld[j % 2], AF.Abs)\
                            .then_inc(self.s_act, 1)
                    if oc < NOC:
                        a.wait_ge(self.s_slot[oc % 2], val["load", oc])
                        if oc == 0 and wait_sw is not None:
                            a.wait_ge(self.s_dve, wait_sw)
                        a.activation(xc[oc % 2], xc[oc % 2], AF.Identity,
                                     scale=swW).then_inc(self.s_act, 1)
                        a.wait_ge(self.s_dve, val["clip", oc])
                        if oc >= 2:
                            a.wait_ge(self.s_te, val["mm", oc - 2, 3])
                        for hh in range(2):
                            sl = slice(hh * 1024, (hh + 1) * 1024)
                            a.activation(self.I32T[:], xc[oc % 2][:, sl],
                                         AF.Identity, bias=self.cols[:, 1:2])\
                                .then_inc(self.s_act, 1)
                            a.activation(wt[oc % 2][:, sl], self.I32T[:],
                                         AF.Identity, bias=self.cols[:, 2:3])\
                                .then_inc(self.s_act, 1)

            @blk.vector
            def _(v):
                for oc in range(NOC + 3):
                    if oc < NOC:
                        v.wait_ge(self.s_act, val["act", oc])
                        ch = xc[oc % 2]
                        v.tensor_scalar(ch, ch, -1.0, 1.0, ALU.max, ALU.min)\
                            .then_inc(self.s_dve, 1)
                    if 2 <= oc <= NOC + 1:
                        ocm = oc - 2
                        for sb in range(4):
                            v.wait_ge(self.s_te, val["mm", ocm, sb])
                            ps = (self.PS + self.PO)[(ocm * 4 + sb) % 6]
                            sbs = slice(sb * 512, (sb + 1) * 512)
                            scl = self.REP[:, 2048 + sb * 512: 2048 + (sb + 1) * 512]
                            if dest == "K":
                                d = self.BBF[:, ocm * 2048 + sb * 512:
                                             ocm * 2048 + (sb + 1) * 512]
                                v.tensor_tensor(d, ps[:], scl, ALU.mult)\
                                    .then_inc(self.s_dve, 1)
                            elif dest in ("Q", "V"):
                                k = ocm * 4 + sb
                                if k >= 2:
                                    v.wait_ge(self.s_gslot[k % 2],
                                              val["stgdma", (k - 2) // 4, (k - 2) % 4])
                                d = self.stg[:, (k % 2) * 512:(k % 2 + 1) * 512]
                                v.tensor_tensor(d, ps[:], scl, ALU.mult)\
                                    .then_inc(self.s_dve, 1)
                            elif dest == "Y1":
                                if sb == 0:
                                    v.wait_ge(self.s_slot[5], val["xload", ocm])
                                v.tensor_tensor(tmp512, ps[:], scl, ALU.mult)
                                v.tensor_tensor(ychunk[:, sbs], tmp512,
                                                xchunk[:, sbs], ALU.add)\
                                    .then_inc(self.s_dve, 1)
                            else:
                                if sb == 0 and ocm >= 1:
                                    v.wait_ge(self.s_gslot[2 + (ocm - 1) % 2],
                                              val["ydma", ocm - 1])
                                v.tensor_single_scalar(
                                    tmp512, ps[:], self.lncol[:, ocm:ocm + 1],
                                    ALU.add)
                                v.tensor_tensor(
                                    ychunk[:, sbs], tmp512,
                                    self.BBF[:, ocm * 2048 + sb * 512:
                                             ocm * 2048 + (sb + 1) * 512],
                                    ALU.add).then_inc(self.s_dve, 1)
                        if dest == "Y1":
                            v.tensor_copy(self.BBF[:, ocm * 2048:(ocm + 1) * 2048],
                                          ychunk).then_inc(self.s_dve, 1)
                    if pm_mm and 3 <= oc and oc - 3 < NOC:
                        j = oc - 3
                        v.wait_ge(self.s_te, val["pmm", j])
                        if j == 0:
                            v.tensor_copy(accrow, self.PX[0:1, 0:512])\
                                .then_inc(self.s_dve, 1)
                        else:
                            v.tensor_tensor(accrow, accrow, self.PX[0:1, 0:512],
                                            ALU.add).then_inc(self.s_dve, 1)

            @blk.tensor
            def _(t):
                for oc in range(NOC + 2):
                    if 1 <= oc <= NOC:
                        ocm = oc - 1
                        for sb in range(4):
                            if sb == 0:
                                t.wait_ge(self.s_act, val["co", ocm, 1])
                            idx = ocm * 4 + sb
                            if idx >= 6:
                                p_idx = idx - 6
                                t.wait_ge(self.s_dve, val["evac", p_idx // 4, p_idx % 4])
                            ps = (self.PS + self.PO)[idx % 6]
                            for ec in range(NCH):
                                ins = t.matmul(
                                    ps[:],
                                    wt[ocm % 2][:, ec * 128:(ec + 1) * 128],
                                    self.ABF[:, ec * 2048 + sb * 512:
                                             ec * 2048 + (sb + 1) * 512],
                                    start=(ec == 0), stop=(ec == NCH - 1))
                            ins.then_inc(self.s_te, 1)
                    if pm_mm and 2 <= oc and oc - 2 < NOC:
                        j = oc - 2
                        t.wait_ge(self.s_act, val["pabs", j])
                        if j >= 1:
                            t.wait_ge(self.s_dve, val["pacc", j - 1])
                        elif wait_sw is not None:
                            t.wait_ge(self.s_dve, wait_sw)
                        for sb in range(4):
                            ins = t.matmul(self.PX[0:1, 0:512],
                                           self.ones_colb_s[:],
                                           absb[:, sb * 512:(sb + 1) * 512],
                                           start=(sb == 0), stop=(sb == 3))
                        ins.then_inc(self.s_te, 1)

            if pm or dest in ("Q", "V", "Y1", "Y2"):
                if dest in ("Q", "V"):
                    dram = self.QT_d if dest == "Q" else self.VT_d

                @blk.gpsimd
                def _(gp):
                    for oc in range(NOC + 2):
                        if dest == "Y1" and oc < NOC:
                            if oc >= 1:
                                gp.wait_ge(self.s_dve, val["evac", oc - 1, 3])
                            else:
                                gp.wait_ge(self.s_dve, dve_entry)
                            gp.dma_start(xchunk,
                                         self.xT[oc * 128:(oc + 1) * 128, :])\
                                .then_inc(self.s_slot[5], 16)
                        if dest in ("Q", "V") and 2 <= oc:
                            ocm = oc - 2
                            for sb in range(4):
                                k = ocm * 4 + sb
                                gp.wait_ge(self.s_dve, val["evac", ocm, sb])
                                gp.dma_start(
                                    dram[ocm * 128:(ocm + 1) * 128,
                                         sb * 512:(sb + 1) * 512],
                                    self.stg[:, (k % 2) * 512:(k % 2 + 1) * 512])\
                                    .then_inc(self.s_gslot[k % 2], 16)
                        if pm and oc < NOC:
                            if oc >= 2:
                                gp.wait_ge(self.s_act,
                                           val["pabs", oc - 2] if pm_abs
                                           else val["co", oc - 2, 1])
                            elif wait_pld is not None:
                                gp.wait_ge(self.s_te, wait_pld)
                            gp.dma_start(pld[oc % 2],
                                         pwd[oc * 128:(oc + 1) * 128, :])\
                                .then_inc(self.s_slot[5 + oc % 2], 16)
                        if dest == "Y2" and 2 <= oc:
                            ocm = oc - 2
                            gp.wait_ge(self.s_dve, val["evac", ocm, 3])
                            gp.dma_start(self.Y2_d[ocm * 128:(ocm + 1) * 128, :],
                                         ychunk).then_inc(
                                self.s_gslot[2 + ocm % 2], 16)
        return val["pacc", NOC - 1] if pm_mm else None

    def p2(self):
        te_v, dve_v = self.weight_mean("k", 1, "attn")
        pmm = self.weight_pass2("k", "K", premean="v",
                                wait_pld=te_v, wait_sw=dve_v)
        te_v, dve_v = self.mean_rows(2, "attn", pmm)
        pmm = self.weight_pass2("v", "V", premean="q",
                                wait_pld=te_v, wait_sw=dve_v)
        te_v, dve_v = self.mean_rows(0, "attn", pmm)
        pmm = self.weight_pass2("q", "Q", premean="o",
                                wait_pld=te_v, wait_sw=dve_v)
        self.mean_rows(3, "attn", pmm, replicate=False, save_sc=45)
        self.gslot_barrier((0, 1))
        self.tap("kp", self.BBF[:, 0:4096], [128, 4096], BF16)
        self.tap("scv", self.sc[0:1, 0:3], [1, 3], F32)

    def p3_prologue(self):
        nc = self.nc
        Vh = self.P3T[:, 0:2304]
        qch = [self.P3T[:, 4352:6400], self.P3T[:, 6400:8448]]
        kabs = self.P3T[:, 8448:10496]
        qsq = self.P3T[:, 2304:2816]

        with nc.Block() as blk:
            @blk.sync
            def _(sync):
                for kc in range(16):
                    sync.dma_start(Vh[:, kc * 144 + 128: kc * 144 + 129],
                                   self.ones_col_b[:]).then_inc(self.s_in, 16)
                    self.n_in += 16
                sync.wait_ge(self.s_in, self.n_in)

            @blk.gpsimd
            def _(gp):
                gp.iota(self.I32T[:, 0:512], [[1, 512]], channel_multiplier=-1,
                        allow_small_or_imprecise_dtypes=True)

            @blk.vector
            def _(v):
                v.tensor_tensor(self.sc[0:1, 37:38], self.sc[0:1, 0:1],
                                self.sc[0:1, 1:2], ALU.mult)
                self.vsync(v)
                v.tensor_scalar_mul(self.sc[0:1, 38:39], self.sc[0:1, 37:38],
                                    INV_SQRT_D)
                self.vsync(v)
                v.tensor_scalar_mul(self.sc[0:1, 36:37], self.sc[0:1, 38:39], -1.0)
                self.vsync(v)
                v.tensor_copy(self.CF32[0:1, 0:128],
                              self.sc[0:1, 38:39].broadcast_to([1, 128]))\
                    .then_inc(self.s_dve, 1)
                self.n_dve += 1
                self.vsync(v)

        b_te = self.n_te
        b_dve0 = self.n_dve
        with nc.Block() as blk:
            @blk.tensor
            def _(t):
                t.wait_ge(self.s_dve, b_dve0)
                t.matmul(self.PX[:, 0:128], self.ones_row_s[:], self.CF32[0:1, 0:128],
                         start=True, stop=True).then_inc(self.s_te, 1)
                self.n_te += 1

            @blk.vector
            def _(v):
                v.wait_ge(self.s_te, b_te + 1)
                v.tensor_copy(self.REP[:, 2048:2176], self.PX[:, 0:128])

        kabs = self.P3T[:, 8448:10496]
        qch = [self.P3T[:, 4352:6400], self.P3T[:, 6400:8448]]
        qsq = [self.P3T[:, 2304:2816], self.P3T[:, 2816:3328]]
        kmaxc = self.Ml_s[:, 0:16]
        val = {}
        vsl = list(self.n_slot)
        vgs = list(self.n_gslot)
        vact, vdve, vte = self.n_act, self.n_dve, self.n_te
        for h in range(H):
            if h % 2 == 0:
                vsl[0] += 16
                val["qld", h] = vsl[0]
            else:
                vgs[2] += 16
                val["qld", h] = vgs[2]
            vact += 1
            val["kab", h] = vact
        for h in range(H + 1):
            if h < H:
                vdve += 1
                val["kred", h] = vdve
            if h >= 1:
                j = h - 1
                for sb in range(4):
                    vdve += 1
                    val["qsq", j, sb] = vdve
                for sb in range(4):
                    vdve += 1
                    val["qrd", j, sb] = vdve
        for h in range(H + 1):
            if h >= 1:
                j = h - 1
                for sb in range(4):
                    vte += 1
                    val["qmm", j, sb] = vte
        self.n_slot = vsl
        self.n_gslot = vgs
        self.n_act, self.n_dve, self.n_te = vact, vdve, vte

        with nc.Block() as blk:
            @blk.sync
            def _(sync):
                sync.wait_ge(self.s_gslot[0], self.n_gslot[0])
                sync.wait_ge(self.s_gslot[1], self.n_gslot[1])
                for h in range(0, H, 2):
                    if h >= 2:
                        sync.wait_ge(self.s_dve, val["qsq", h - 2, 3])
                    sync.dma_start(qch[0], self.QT_d[h * 128:(h + 1) * 128, :])\
                        .then_inc(self.s_slot[0], 16)

            @blk.gpsimd
            def _(gp):
                for h in range(1, H, 2):
                    if h >= 3:
                        gp.wait_ge(self.s_dve, val["qsq", h - 2, 3])
                    gp.dma_start(qch[1], self.QT_d[h * 128:(h + 1) * 128, :])\
                        .then_inc(self.s_gslot[2], 16)

            @blk.scalar
            def _(a):
                for h in range(H):
                    if h >= 1:
                        a.wait_ge(self.s_dve, val["kred", h - 1])
                    a.activation(kabs, self.BBF[:, h * 2048:(h + 1) * 2048],
                                 AF.Abs).then_inc(self.s_act, 1)

            @blk.vector
            def _(v):
                for h in range(H + 1):
                    if h < H:
                        v.wait_ge(self.s_act, val["kab", h])
                        v.reduce_max(kmaxc[:, h:h + 1], kabs, AX.X)\
                            .then_inc(self.s_dve, 1)
                    if h >= 1:
                        j = h - 1
                        v.wait_ge(self.s_slot[0] if j % 2 == 0
                                  else self.s_gslot[2], val["qld", j])
                        for sb in range(4):
                            if (j, sb) > (0, 1):
                                ps = (j, sb - 2) if sb >= 2 else (j - 1, sb + 2)
                                v.wait_ge(self.s_te, val["qmm", ps[0], ps[1]])
                            v.tensor_tensor(
                                qsq[sb % 2],
                                qch[j % 2][:, sb * 512:(sb + 1) * 512],
                                qch[j % 2][:, sb * 512:(sb + 1) * 512],
                                ALU.mult).then_inc(self.s_dve, 1)
                        for sb in range(4):
                            v.wait_ge(self.s_te, val["qmm", j, sb])
                            v.reduce_max(self.sc[0:1, 40:41],
                                         self.PO[sb][0:1, 0:512], AX.X)\
                                .then_inc(self.s_dve, 1)
                            self.vsync(v)
                            if sb == 0:
                                v.tensor_copy(self.sc[0:1, 48 + j:49 + j],
                                              self.sc[0:1, 40:41])
                            else:
                                v.tensor_tensor(self.sc[0:1, 48 + j:49 + j],
                                                self.sc[0:1, 48 + j:49 + j],
                                                self.sc[0:1, 40:41], ALU.max)
                            self.vsync(v)

            @blk.tensor
            def _(t):
                for h in range(H + 1):
                    if h >= 1:
                        j = h - 1
                        for sb in range(4):
                            t.wait_ge(self.s_dve, val["qsq", j, sb])
                            if j >= 1:
                                t.wait_ge(self.s_dve, val["qrd", j - 1, sb])
                            t.matmul(self.PO[sb][0:1, 0:512],
                                     self.ones_colb_s[:], qsq[sb % 2],
                                     start=True, stop=True)\
                                .then_inc(self.s_te, 1)

        b_te, b_dve, b_act = self.n_te, self.n_dve, self.n_act
        with nc.Block() as blk:
            @blk.vector
            def _(v):
                self.vsync(v)
                v.tensor_tensor(self.Ml_s[:, 16:32], kmaxc, kmaxc, ALU.mult)\
                    .then_inc(self.s_dve, 1)
                self.n_dve += 1
                v.wait_ge(self.s_te, b_te + 1)
                v.tensor_copy(self.sc[0:1, 16:32], self.PX[0:1, 0:16])
                self.vsync(v)
                v.tensor_scalar_add(self.sc[0:1, 16:32], self.sc[0:1, 16:32],
                                    1e-30)
                self.vsync(v)
                v.tensor_scalar_add(self.sc[0:1, 48:64], self.sc[0:1, 48:64],
                                    1e-30)
                self.vsync(v)
                v.nop().then_inc(self.s_dve, 1)
                self.n_dve += 1
                v.wait_ge(self.s_act, b_act + 2)
                v.tensor_tensor(self.sc[0:1, 48:64], self.sc[0:1, 48:64],
                                self.sc[0:1, 16:32], ALU.add)
                self.vsync(v)
                v.nop().then_inc(self.s_dve, 1)
                self.n_dve += 1
                v.wait_ge(self.s_act, b_act + 3)
                v.tensor_single_scalar(self.sc[0:1, 48:64],
                                       self.sc[0:1, 48:64],
                                       self.sc[0:1, 36:37], ALU.mult)
                self.vsync(v)
                v.tensor_copy(self.CF32[0:1, 0:16], self.sc[0:1, 48:64])\
                    .then_inc(self.s_dve, 1)
                self.n_dve += 1
                v.wait_ge(self.s_te, b_te + 2)
                v.tensor_copy(self.REP[:, 0:16], self.PX[:, 0:16])\
                    .then_inc(self.s_dve, 1)
                self.n_dve += 1

            @blk.scalar
            def _(a):
                a.wait_ge(self.s_dve, b_dve + 2)
                a.activation(self.sc[0:1, 16:32], self.sc[0:1, 16:32], AF.Ln)\
                    .then_inc(self.s_act, 1)
                a.activation(self.sc[0:1, 48:64], self.sc[0:1, 48:64], AF.Ln)\
                    .then_inc(self.s_act, 1)
                self.n_act += 2
                a.wait_ge(self.s_dve, b_dve + 3)
                a.activation(self.sc[0:1, 48:64], self.sc[0:1, 48:64],
                             AF.Exp, scale=0.5).then_inc(self.s_act, 1)
                self.n_act += 1

            @blk.tensor
            def _(t):
                t.wait_ge(self.s_dve, b_dve + 1)
                t.matmul(self.PX[0:1, 0:16], self.ones_colf_s[:],
                         self.Ml_s[:, 16:32], start=True, stop=True)\
                    .then_inc(self.s_te, 1)
                self.n_te += 1
                t.wait_ge(self.s_dve, b_dve + 4)
                t.matmul(self.PX[:, 0:16], self.ones_row_s[:],
                         self.CF32[0:1, 0:16], start=True, stop=True)\
                    .then_inc(self.s_te, 1)
                self.n_te += 1
        self.p3_bias_guard = self.n_dve

        with nc.Block() as blk:
            @blk.vector
            def _(v):
                for j in range(4):
                    v.tensor_scalar(self.P3T[:, 8448 + j * 512: 8448 + (j + 1) * 512],
                                    self.I32T[:, 0:512], float(128 * j), None,
                                    ALU.is_ge)

    def p3_main(self):
        nc = self.nc
        Vh = self.P3T[:, 0:2304]
        PT = [self.P3T[:, 2304:2816], self.P3T[:, 2816:3328],
              self.stg[:, 0:512], self.stg[:, 512:1024]]
        QKPS = [self.PS[0], self.PS[1], self.PX]
        Qh = [self.P3T[:, 3328:3840], self.P3T[:, 3840:4352]]
        VhFM = [self.P3T[:, 4352:6400], self.P3T[:, 6400:8448]]
        oqp = self.P3T[:, 4352:5376]
        masks = self.P3T[:, 8448:10496]
        O = self.CF32
        Ml = self.Ml_s
        cvec = self.REP[:, 2048:2049]

        val = {}
        vte, vact, vdve = self.n_te, self.n_act, self.n_dve
        vsl = list(self.n_slot)
        for qt in range(4):
            nkc = 4 * qt + 4
            for h in range(H):
                vsl[h % 2] += 16
                val["ldq", qt, h] = vsl[h % 2]
                vsl[2 + h % 2] += 16
                val["ldv", qt, h] = vsl[2 + h % 2]
            for h in range(H):
                ng = (nkc + 3) // 4
                for g in range(ng):
                    vte += 1
                    val["vtr", qt, h, g] = vte
                    vdve += 1
                    val["vev", qt, h, g] = vdve
                for kc in range(nkc):
                    vte += 1
                    val["qk", qt, h, kc] = vte
                    if kc >= 1:
                        vte += 1
                    val["av", qt, h, kc - 1] = vte
                vte += 1
                val["av", qt, h, nkc - 1] = vte
                for kc in range(nkc):
                    vact += 1
                    val["exp", qt, h, kc] = vact
                    if kc >= 4 * qt:
                        vdve += 1
                        val["mask", qt, h, kc] = vdve
                vdve += 1
                val["oev", qt, h] = vdve
            vdve += 1
            val["ebmax", qt] = vdve
            vdve += 1
            val["eso2", qt] = vdve
            for piece in range(8):
                vdve += 1
                val["emul", qt, piece] = vdve
                vact += 1
                val["eci", qt, piece] = vact
                vact += 1
                val["equant", qt, piece] = vact
                for grp in range(2):
                    vte += 1
                    val["etr", qt, piece, grp] = vte
                    vdve += 1
                    val["eev", qt, piece, grp] = vdve

        self.n_te, self.n_act, self.n_dve = vte, vact, vdve
        self.n_slot = vsl

        def diag(qt, kc):
            return kc - 4 * qt

        with nc.Block() as blk:
            @blk.sync
            def _(sync):
                for qt in range(4):
                    nkc = 4 * qt + 4
                    for h in range(H):
                        if (qt, h) >= (0, 2):
                            ph = (qt, h - 2) if h >= 2 else (qt - 1, H - 2 + h)
                            sync.wait_ge(self.s_te,
                                         val["av", ph[0], ph[1],
                                             4 * ph[0] + 4 - 1])
                        sync.dma_start(Qh[h % 2],
                                       self.QT_d[h * 128:(h + 1) * 128,
                                                 qt * 512:(qt + 1) * 512])\
                            .then_inc(self.s_slot[h % 2], 16)
                        if (qt, h) >= (0, 2):
                            ph = (qt, h - 2) if h >= 2 else (qt - 1, H - 2 + h)
                            png = (4 * ph[0] + 4 + 3) // 4
                            sync.wait_ge(self.s_te,
                                         val["vtr", ph[0], ph[1], png - 1])
                        if h == 0 and qt > 0:
                            sync.wait_ge(self.s_te, val["etr", qt - 1, 7, 1])
                        if (qt, h) == (0, 0):
                            sync.wait_ge(self.s_dve, self.p3_bias_guard)
                        sync.dma_start(VhFM[h % 2][:, 0:nkc * 128],
                                       self.VT_d[h * 128:(h + 1) * 128,
                                                 0:nkc * 128])\
                            .then_inc(self.s_slot[2 + h % 2], 16)

            @blk.tensor
            def _(t):
                for qt in range(4):
                    nkc = 4 * qt + 4
                    for h in range(H):
                        ng = (nkc + 3) // 4
                        for g in range(ng):
                            t.wait_ge(self.s_slot[2 + h % 2], val["ldv", qt, h])
                            if g > 0:
                                t.wait_ge(self.s_dve, val["vev", qt, h, g - 1])
                            elif h > 0:
                                t.wait_ge(self.s_dve, val["vev", qt, h - 1, ng - 1])
                            elif qt > 0:
                                t.wait_ge(self.s_dve, val["eev", qt - 1, 7, 1])
                            for j in range(4):
                                kc = g * 4 + j
                                if kc >= nkc:
                                    break
                                ins = t.transpose(
                                    self.PSB[:, j * 128:(j + 1) * 128],
                                    VhFM[h % 2][:, kc * 128:(kc + 1) * 128],
                                    self.identb_s[:])
                            ins.then_inc(self.s_te, 1)
                        for kc in range(nkc):
                            t.wait_ge(self.s_slot[h % 2], val["ldq", qt, h])
                            if kc >= 3:
                                t.wait_ge(self.s_act, val["exp", qt, h, kc - 3])
                            elif (qt, h) > (0, 0):
                                ph = (qt, h - 1) if h else (qt - 1, H - 1)
                                pn = 4 * ph[0] + 4
                                t.wait_ge(self.s_act,
                                          val["exp", ph[0], ph[1], pn - 3 + kc])
                            t.matmul(QKPS[kc % 3][:],
                                     self.BBF[:, h * 2048 + kc * 128:
                                              h * 2048 + (kc + 1) * 128],
                                     Qh[h % 2], start=True, stop=True)\
                                .then_inc(self.s_te, 1)
                            if kc >= 1:
                                self._p3_av(t, val, qt, h, kc - 1)
                        self._p3_av(t, val, qt, h, nkc - 1)
                    for piece in range(8):
                        for grp in range(2):
                            t.wait_ge(self.s_act, val["equant", qt, piece])
                            pg = (qt, piece, grp - 1) if grp else \
                                 ((qt, piece - 1, 1) if piece else None)
                            if pg is None and qt:
                                pg = (qt - 1, 7, 1)
                            if pg:
                                t.wait_ge(self.s_dve, val["eev", pg[0], pg[1], pg[2]])
                            for j in range(4):
                                blkidx = piece * 8 + grp * 4 + j
                                ins = t.transpose(
                                    self.PSB[:, j * 128:(j + 1) * 128],
                                    oqp[:, (grp * 4 + j) * 128:
                                        (grp * 4 + j + 1) * 128],
                                    self.identb_s[:])
                            ins.then_inc(self.s_te, 1)

            @blk.scalar
            def _(a):
                for qt in range(4):
                    nkc = 4 * qt + 4
                    for h in range(H):
                        for kc in range(nkc):
                            a.wait_ge(self.s_te, val["qk", qt, h, kc])
                            if kc >= 4:
                                a.wait_ge(self.s_te, val["av", qt, h, kc - 4])
                            elif (qt, h) > (0, 0):
                                ph = (qt, h - 1) if h else (qt - 1, H - 1)
                                pn = 4 * ph[0] + 4
                                a.wait_ge(self.s_te,
                                          val["av", ph[0], ph[1], pn - 4 + kc])
                            if (qt, h, kc) == (0, 0, 0):
                                a.wait_ge(self.s_dve, self.p3_bias_guard)
                            a.activation(PT[kc % 4], QKPS[kc % 3][:], AF.Exp,
                                         scale=cvec,
                                         bias=self.REP[:, h:h + 1])\
                                .then_inc(self.s_act, 1)
                    for piece in range(8):
                        a.wait_ge(self.s_dve, val["emul", qt, piece])
                        pg = (qt, piece - 1) if piece else \
                             ((qt - 1, 7) if qt else None)
                        if pg is not None:
                            a.wait_ge(self.s_te, val["etr", pg[0], pg[1], 1])
                        Opc = O[:, piece * 1024:(piece + 1) * 1024]
                        a.activation(self.I32T[:], Opc, AF.Identity,
                                     bias=self.cols[:, 1:2])\
                            .then_inc(self.s_act, 1)
                        a.activation(oqp, self.I32T[:], AF.Identity,
                                     bias=self.cols[:, 2:3])\
                            .then_inc(self.s_act, 1)

            @blk.vector
            def _(v):
                for qt in range(4):
                    nkc = 4 * qt + 4
                    for h in range(H):
                        ng = (nkc + 3) // 4
                        for g in range(ng):
                            v.wait_ge(self.s_te, val["vtr", qt, h, g])
                            psrc = self.PSB[:, 0:512].rearrange(
                                "p (t c) -> p t c", c=128)
                            pdst = Vh[:, g * 576:(g + 1) * 576].rearrange(
                                "p (t c) -> p t c", c=144)[:, :, 0:128]
                            v.tensor_copy(pdst, psrc).then_inc(self.s_dve, 1)
                        for kc in range(4 * qt, nkc):
                            v.wait_ge(self.s_act, val["exp", qt, h, kc])
                            j = diag(qt, kc)
                            v.tensor_tensor(PT[kc % 4], PT[kc % 4],
                                            masks[:, j * 512:(j + 1) * 512],
                                            ALU.mult).then_inc(self.s_dve, 1)
                        v.wait_ge(self.s_te, val["av", qt, h, nkc - 1])
                        for qcL in range(4):
                            v.tensor_copy(
                                O[:, h * 512 + qcL * 128: h * 512 + (qcL + 1) * 128],
                                self.PO[qcL][:, 0:128])
                            ins = v.tensor_copy(
                                Ml[:, 4 * h + qcL: 4 * h + qcL + 1],
                                self.PO[qcL][:, 128:129])
                        ins.then_inc(self.s_dve, 1)
                    v.tensor_reduce(Ml[:, 64:128].rearrange("p (t c) -> p t c", c=1),
                                    O[:].rearrange("p (t c) -> p t c", c=128),
                                    AX.X, ALU.max, apply_absolute_value=True)\
                        .then_inc(self.s_dve, 1)
                    self.vsync(v)
                    v.reciprocal(Ml[:, 128:192], Ml[:, 0:64])
                    self.vsync(v)
                    v.tensor_tensor(Ml[:, 64:128], Ml[:, 64:128], Ml[:, 128:192],
                                    ALU.mult)
                    self.vsync(v)
                    v.tensor_reduce(
                        self.so2col[:, 4 * qt: 4 * qt + 4]
                        .rearrange("p (t c) -> p t c", c=1),
                        Ml[:, 64:128].rearrange("p (h q) -> p q h", h=16, q=4),
                        AX.X, ALU.max)
                    self.vsync(v)
                    v.tensor_single_scalar(self.so2col[:, 4 * qt: 4 * qt + 4],
                                           self.so2col[:, 4 * qt: 4 * qt + 4],
                                           1e-5, ALU.max)
                    self.vsync(v)
                    v.reciprocal(Ml[:, 64:68], self.so2col[:, 4 * qt:4 * qt + 4])
                    self.vsync(v)
                    v.tensor_single_scalar(Ml[:, 64:68],
                                           Ml[:, 64:68], 127.0, ALU.mult)
                    self.vsync(v)
                    v.tensor_tensor(
                        Ml[:, 128:192].rearrange("p (h q) -> p h q", h=16),
                        Ml[:, 128:192].rearrange("p (h q) -> p h q", h=16),
                        Ml[:, 64:68].rearrange("p (o q) -> p o q", o=1)
                        .broadcast_to([128, 16, 4]),
                        ALU.mult).then_inc(self.s_dve, 1)
                    self.vsync(v)
                    for piece in range(8):
                        Opc = O[:, piece * 1024:(piece + 1) * 1024]
                        sl2 = Ml[:, 128 + piece * 8: 128 + (piece + 1) * 8]
                        v.tensor_tensor(
                            Opc.rearrange("p (b c) -> p b c", c=128),
                            Opc.rearrange("p (b c) -> p b c", c=128),
                            sl2.rearrange("p (b o) -> p b o", o=1)
                            .broadcast_to([128, 8, 128]), ALU.mult)\
                            .then_inc(self.s_dve, 1)
                        for grp in range(2):
                            v.wait_ge(self.s_te, val["etr", qt, piece, grp])
                            blk0 = piece * 8 + grp * 4
                            h0, q0 = blk0 // 4, blk0 % 4
                            dst = self.ABF[:, h0 * 2048 + (4 * qt + q0) * 128:
                                           h0 * 2048 + (4 * qt + q0) * 128 + 512]
                            v.tensor_copy(dst, self.PSB[:])\
                                .then_inc(self.s_dve, 1)

    def _p3_av(self, t, val, qt, h, kc):
        nkc = 4 * qt + 4
        if kc < 0:
            return
        if kc >= 4 * qt:
            t.wait_ge(self.s_dve, val["mask", qt, h, kc])
        else:
            t.wait_ge(self.s_act, val["exp", qt, h, kc])
        t.wait_ge(self.s_dve, val["vev", qt, h, kc // 4])
        if kc == 0 and (qt, h) > (0, 0):
            ph = (qt, h - 1) if h else (qt - 1, H - 1)
            t.wait_ge(self.s_dve, val["oev", ph[0], ph[1]])
        Vh = self.P3T[:, 0:2304]
        PT = [self.P3T[:, 2304:2816], self.P3T[:, 2816:3328],
              self.stg[:, 0:512], self.stg[:, 512:1024]]
        ins = None
        for qcL in range(4):
            if 4 * qt + qcL < kc:
                continue
            ins = t.matmul(
                self.PO[qcL][:, 0:129],
                PT[kc % 4][:, qcL * 128:(qcL + 1) * 128],
                Vh[:, kc * 144: kc * 144 + 129],
                start=(kc == 0), stop=(kc == 4 * qt + qcL))
        if ins is not None:
            ins.then_inc(self.s_te, 1)

    def p3(self):
        self.p3_prologue()
        self.p3_main()
        self.tap("oq", self.ABF[:, 0:4096], [128, 4096], BF16)
        self.tap("so2", self.so2col[:], [128, 16], F32)
        self.tap("biasw", self.REP[:, 0:2048], [128, 2048], F32)

    def stats_v2(self, mode, rows_at=0):
        nc = self.nc
        C = self.CF32
        ysq = [self.P3T[:, 0:1024], self.P3T[:, 1024:2048]]
        ybf = [self.P3T[:, 2048:3072], self.P3T[:, 3072:4096]]
        for w in range(2):
            val = {}
            vte, vdve = self.n_te, self.n_dve
            vsl7 = self.n_slot[7]
            for oc in range(NCH):
                if mode == "stream":
                    vsl7 += 16
                    val["ld", oc] = vsl7
                vdve += 1
                val["sq", oc] = vdve
                vte += 1
                val["mm", oc] = vte
            self.n_te, self.n_dve = vte, vdve
            self.n_slot[7] = vsl7
            with nc.Block() as blk:
                if mode == "stream":
                    @blk.sync
                    def _(sync):
                        for oc in range(NCH):
                            if oc >= 2:
                                sync.wait_ge(self.s_dve, val["sq", oc - 2])
                            sync.dma_start(
                                yc0[oc % 2] if False else
                                C[:, (oc % 2) * 2048:(oc % 2) * 2048 + 1024],
                                self.Y2_d[oc * 128:(oc + 1) * 128,
                                          w * 1024:(w + 1) * 1024])\
                                .then_inc(self.s_slot[7], 16)

                @blk.vector
                def _(v):
                    for oc in range(NCH):
                        if mode == "stream":
                            v.wait_ge(self.s_slot[7], val["ld", oc])
                            srcw = ybf[oc % 2]
                            v.tensor_copy(srcw,
                                          C[:, (oc % 2) * 2048:(oc % 2) * 2048 + 1024])
                        else:
                            srcw = self.BBF[:, oc * 2048 + w * 1024:
                                            oc * 2048 + (w + 1) * 1024]
                        if oc >= 2:
                            v.wait_ge(self.s_te, val["mm", oc - 2])
                        v.tensor_tensor(ysq[oc % 2], srcw, srcw, ALU.mult)\
                            .then_inc(self.s_dve, 1)

                @blk.tensor
                def _(t):
                    for oc in range(NCH):
                        t.wait_ge(self.s_dve, val["sq", oc])
                        if mode == "stream":
                            srcw = ybf[oc % 2]
                        else:
                            srcw = self.BBF[:, oc * 2048 + w * 1024:
                                            oc * 2048 + (w + 1) * 1024]
                        for half in range(2):
                            hs = slice(half * 512, (half + 1) * 512)
                            t.matmul(self.PO[half][0:1, 0:512], self.ones_colb_s[:],
                                     srcw[:, hs], start=(oc == 0),
                                     stop=(oc == NCH - 1))
                            ins = t.matmul(self.PO[2 + half][0:1, 0:512],
                                           self.ones_colb_s[:],
                                           ysq[oc % 2][:, hs], start=(oc == 0),
                                           stop=(oc == NCH - 1))
                        ins.then_inc(self.s_te, 1)
            b_te = self.n_te
            with nc.Block() as blk:
                @blk.vector
                def _(v):
                    for half in range(2):
                        col = rows_at + w * 1024 + half * 512
                        v.tensor_copy(C[0:1, col:col + 512],
                                      self.PO[half][0:1, 0:512])
                        v.tensor_copy(C[0:1, 2048 + col:2048 + col + 512],
                                      self.PO[2 + half][0:1, 0:512])

    def ln_rows_and_reps(self, eps=1e-5, rows_at=0, mrow_at=4096):
        nc = self.nc
        C = self.CF32
        srow = C[0:1, rows_at:rows_at + 2048]
        qrow = C[0:1, rows_at + 2048:rows_at + 4096]
        mrow = C[0:1, mrow_at:mrow_at + 2048]
        b_act, b_dve = self.n_act, self.n_dve
        with nc.Block() as blk:
            @blk.vector
            def _(v):
                v.tensor_scalar_mul(srow, srow, 1.0 / E)
                v.tensor_scalar_mul(qrow, qrow, 1.0 / E)
                self.vsync(v)
                v.tensor_tensor(mrow, srow, srow, ALU.mult)
                self.vsync(v)
                v.tensor_tensor(qrow, qrow, mrow, ALU.subtract)
                self.vsync(v)
                v.tensor_copy(C[0:1, 6144:8192], qrow)
                self.vsync(v)
                v.tensor_scalar_add(qrow, qrow, eps)
                self.vsync(v)
                v.nop().then_inc(self.s_dve, 1)
                self.n_dve += 1
                v.wait_ge(self.s_act, b_act + 2)
                v.tensor_tensor(mrow, srow, qrow, ALU.mult)
                self.vsync(v)
                v.tensor_copy(srow, qrow)
                self.vsync(v)

            @blk.scalar
            def _(a):
                a.wait_ge(self.s_dve, b_dve + 1)
                a.activation(qrow, qrow, AF.Ln).then_inc(self.s_act, 1)
                self.n_act += 1
                self.asynch(a)
                a.activation(qrow, qrow, AF.Exp, scale=-0.5)\
                    .then_inc(self.s_act, 1)
                self.n_act += 1
        self.replicate_row(srow, self.REP[:, 2048:4096])
        self.replicate_row(mrow, self.REP[:, 0:2048])

    def gslot_barrier(self, idxs):
        with self.nc.Block() as blk:
            @blk.gpsimd
            def _(gp):
                for i in idxs:
                    gp.wait_ge(self.s_gslot[i], self.n_gslot[i])

    def p4(self):
        nc = self.nc
        with nc.Block() as blk:
            @blk.vector
            def _(v):
                v.tensor_copy(self.CF32[0:1, 5248:5376],
                              self.sc[0:1, 45:46].broadcast_to([1, 128]))\
                    .then_inc(self.s_dve, 1)
                self.n_dve += 1
        _, dve_sw = self.sw_tail()
        with nc.Block() as blk:
            @blk.sync
            def _(sync):
                rt = self.rowtmp_d[:].rearrange("o (c p) -> p (o c)", p=128, c=16)
                sync.dma_start(rt, self.so2col[:], allow_slow_non_contiguous=True)\
                    .then_inc(self.s_in, 16)
                self.n_in += 16
                sync.wait_ge(self.s_in, self.n_in)
                sync.dma_start(self.CF32[0:1, 0:2048], self.rowtmp_d[:])\
                    .then_inc(self.s_in, 16)
                self.n_in += 16
                sync.wait_ge(self.s_in, self.n_in)
        with nc.Block() as blk:
            @blk.vector
            def _(v):
                v.tensor_tensor(self.sc[0:1, 44:45], self.sc[0:1, 2:3],
                                self.sc[0:1, 3:4], ALU.mult)
                self.vsync(v)
                v.tensor_scalar_mul(self.sc[0:1, 44:45], self.sc[0:1, 44:45],
                                    1.0 / 127.0)
                self.vsync(v)
                v.tensor_single_scalar(self.CF32[0:1, 0:2048],
                                       self.CF32[0:1, 0:2048],
                                       self.sc[0:1, 44:45], ALU.mult)
                self.vsync(v)
        self.replicate_row(self.CF32[0:1, 0:2048], self.REP[:, 2048:4096])
        self.weight_pass2("o", "Y1", wait_sw=dve_sw)
        self.tap("y1b", self.BBF[:, 0:4096], [128, 4096], BF16)
        self.stats_v2("resident")
        self.ln_rows_and_reps()
        C = self.CF32
        b_act = self.n_act
        b_dve = self.n_dve
        with nc.Block() as blk:
            @blk.vector
            def _(v):
                vr = C[0:1, 6144:8192]
                t2 = C[0:1, 4096:6144]
                v.tensor_scalar_add(t2, vr, 1e-5)
                self.vsync(v)
                v.nop().then_inc(self.s_dve, 1)
                self.n_dve += 1
                v.wait_ge(self.s_act, b_act + 2)
                v.tensor_tensor(t2, vr, t2, ALU.mult)
                self.vsync(v)
                v.tensor_scalar_add(t2, t2, 1e-5)
                self.vsync(v)
                v.nop().then_inc(self.s_dve, 1)
                self.n_dve += 1

            @blk.scalar
            def _(a):
                t2 = C[0:1, 4096:6144]
                a.wait_ge(self.s_dve, b_dve + 1)
                a.activation(t2, t2, AF.Ln).then_inc(self.s_act, 1)
                self.n_act += 1
                self.asynch(a)
                a.activation(t2, t2, AF.Exp, scale=-1.0)\
                    .then_inc(self.s_act, 1)
                self.n_act += 1
                a.wait_ge(self.s_dve, b_dve + 2)
                a.activation(t2, t2, AF.Ln).then_inc(self.s_act, 1)
                self.n_act += 1
                self.asynch(a)
                a.activation(t2, t2, AF.Exp, scale=-0.5)\
                    .then_inc(self.s_act, 1)
                self.n_act += 1
        self.replicate_row(C[0:1, 4096:6144], C[:, 0:2048])
        val2 = {}
        vdve, vact = self.n_dve, self.n_act
        for oc in range(NCH):
            vdve += 1
            val2["nrm", oc] = vdve
            vact += 1
            val2["gb", oc] = vact
        self.n_dve, self.n_act = vdve, vact
        t32 = [C[:, 4096:6144], C[:, 6144:8192]]
        with nc.Block() as blk:
            @blk.vector
            def _(v):
                for oc in range(NCH):
                    if oc >= 2:
                        v.wait_ge(self.s_act, val2["gb", oc - 2])
                    ch = self.BBF[:, oc * 2048:(oc + 1) * 2048]
                    tb = t32[oc % 2]
                    v.tensor_tensor(tb, ch, self.REP[:, 2048:4096], ALU.mult)
                    v.tensor_tensor(ch, tb, self.REP[:, 0:2048], ALU.subtract)
                    v.tensor_tensor(tb, ch, C[:, 0:2048], ALU.mult)\
                        .then_inc(self.s_dve, 1)

            @blk.scalar
            def _(a):
                for oc in range(NCH):
                    a.wait_ge(self.s_dve, val2["nrm", oc])
                    a.activation(self.ABF[:, oc * 2048:(oc + 1) * 2048],
                                 t32[oc % 2], AF.Identity,
                                 scale=self.lncol[:, 16 + oc:17 + oc],
                                 bias=self.lncol[:, 32 + oc:33 + oc])\
                        .then_inc(self.s_act, 1)
        self.tap("r1", self.BBF[:, 0:4096], [128, 4096], BF16)
        self.tap("xn", self.ABF[:, 0:4096], [128, 4096], BF16)

    def p5(self):
        nc = self.nc
        C = self.CF32
        _, dve_sw = self.weight_mean("f", 4, "ffn")
        self.weight_pass2("f", "Y2", wait_sw=dve_sw)
        self.gslot_barrier((2, 3))
        self.stats_v2("stream", rows_at=4096)
        self.ln_rows_and_reps(rows_at=4096, mrow_at=0)
        val2 = {}
        vdve = self.n_dve
        vgs = list(self.n_gslot)
        vsl6 = self.n_slot[6]
        for oc in range(NCH):
            vsl6 += 16
            val2["yld", oc] = vsl6
            vdve += 1
            val2["norm", oc] = vdve
            vgs[oc % 2] += 16
            val2["odma", oc] = vgs[oc % 2]
        self.n_dve = vdve
        self.n_gslot = vgs
        self.n_slot[6] = vsl6
        yc = [C[:, 0:2048], C[:, 2048:4096]]
        t32 = [C[:, 4096:6144], C[:, 6144:8192]]
        with nc.Block() as blk:
            @blk.sync
            def _(sync):
                for oc in range(NCH):
                    if oc >= 2:
                        sync.wait_ge(self.s_dve, val2["norm", oc - 2])
                    sync.dma_start(yc[oc % 2], self.Y2_d[oc * 128:(oc + 1) * 128, :])\
                        .then_inc(self.s_slot[6], 16)

            @blk.vector
            def _(v):
                for oc in range(NCH):
                    v.wait_ge(self.s_slot[6], val2["yld", oc])
                    if oc >= 2:
                        v.wait_ge(self.s_gslot[oc % 2], val2["odma", oc - 2])
                    v.tensor_tensor(t32[oc % 2], yc[oc % 2],
                                    self.REP[:, 2048:4096], ALU.mult)
                    v.tensor_tensor(t32[oc % 2], t32[oc % 2],
                                    self.REP[:, 0:2048], ALU.subtract)\
                        .then_inc(self.s_dve, 1)

            @blk.gpsimd
            def _(gp):
                for oc in range(NCH):
                    gp.wait_ge(self.s_dve, val2["norm", oc])
                    gp.dma_start(self.y[oc * 128:(oc + 1) * 128, :], t32[oc % 2])\
                        .then_inc(self.s_gslot[oc % 2], 16)
        self.gslot_barrier((0, 1))

    def finalize(self):
        self.nc.finalize()
        return self.nc


def build(debug=(), phases=5):
    mk = MK(debug)
    mk.p0()
    mk.p1()
    if phases >= 2:
        mk.p2()
    if phases >= 3:
        mk.p3()
    if phases >= 4:
        mk.p4()
    if phases >= 5:
        mk.p5()
    return mk


def _strip_tiled(w):
    a = np.asarray(w).reshape(16, 128, 16, 128).transpose(0, 3, 2, 1)
    return np.ascontiguousarray(a).reshape(2048, 2048)


def host_inputs(inputs, core):
    x = inputs["L0_attn_x"]
    m = {
        "xT": np.ascontiguousarray(np.asarray(x[core]).T),
        "w_q": _strip_tiled(inputs["L0_attn_w_q"]),
        "w_k": _strip_tiled(inputs["L0_attn_w_k"]),
        "w_v": _strip_tiled(inputs["L0_attn_w_v"]),
        "w_o": _strip_tiled(inputs["L0_attn_w_o"]),
        "w_f": _strip_tiled(inputs["L0_ffn_weight"]),
        "bias_col": np.ascontiguousarray(np.asarray(inputs["L0_ffn_bias"]).reshape(16, 128).T.astype(np.float32)),
        "gamma_col": np.ascontiguousarray(np.asarray(inputs["L0_ffn_ln_gamma"]).reshape(16, 128).T.astype(np.float32)),
        "beta_col": np.ascontiguousarray(np.asarray(inputs["L0_ffn_ln_beta"]).reshape(16, 128).T.astype(np.float32)),
        "ones_row": np.ones((1, 128), np.float32),
        "ones_col_b": np.ones((128, 1), ml_dtypes.bfloat16),
        "ones_col_f": np.ones((128, 1), np.float32),
        "ident_f": np.eye(128, dtype=np.float32),
        "ident_b": np.eye(128, dtype=ml_dtypes.bfloat16),
        "c07_col": np.full((128, 1), 0.7, np.float32),
    }
    return m


from concourse.bass_utils import run_bass_kernel_spmd

N_CORES = 8


def kernel(**inputs):
    mkb = build(debug=(), phases=5)
    nc = mkb.finalize()
    in_maps = [host_inputs(inputs, c) for c in range(N_CORES)]
    res = run_bass_kernel_spmd(nc, in_maps, core_ids=list(range(N_CORES)))
    out = np.empty((N_CORES, S, E), np.float32)
    for c in range(N_CORES):
        out[c] = np.asarray(res.results[c]["y"]).T
    return out
